# version 02 (final)
# baseline (speedup 1.0000x reference)
import math
import numpy as np
import ml_dtypes
import concourse.bass as bass
import concourse.mybir as mybir
from concourse.bass_utils import run_bass_kernel_spmd

F32 = mybir.dt.float32
BF16 = mybir.dt.bfloat16
I32 = mybir.dt.int32
U8 = mybir.dt.uint8
AF = mybir.ActivationFunctionType
ALU = mybir.AluOpType
PI = math.pi

DM, DI, NH, HD, NK = 1024, 2048, 32, 64, 32
CONVD, DIP, CH, LORA, NEXP = 2176, 4256, 64, 48, 8
RMS_EPS = LN_EPS = 1e-5
NHK_T = 11


def hk_rows(j):
    return 96 if j < 10 else 64


def heads_of(j):
    return [3 * j + i for i in range(3) if 3 * j + i < NH]


def host_prepare(inputs, TT=256):
    W = np.asarray(inputs["L0_attn_in_proj_w"], dtype=np.float32)
    conv_w = np.asarray(inputs["L0_attn_conv_w"], dtype=np.float32)
    conv_b = np.asarray(inputs["L0_attn_conv_b"], dtype=np.float32)
    B0 = 2 * DI
    idx = (list(range(0, DI)) + list(range(DI, 2 * DI))
           + list(range(B0, B0 + 64, 2)) + list(range(B0 + 1, B0 + 64, 2))
           + list(range(B0 + 64, B0 + 128, 2)) + list(range(B0 + 65, B0 + 128, 2))
           + list(range(B0 + 128, B0 + 128 + NH)))
    Wp = np.ascontiguousarray(W[idx])
    cidx = (list(range(0, DI))
            + list(range(DI, DI + 64, 2)) + list(range(DI + 1, DI + 64, 2))
            + list(range(DI + 64, DI + 128, 2)) + list(range(DI + 65, DI + 128, 2)))
    conv_wt = np.ascontiguousarray(conv_w[cidx].reshape(17, 128, 4).transpose(1, 0, 2))
    conv_bt = np.ascontiguousarray(conv_b[cidx].reshape(17, 128).T)

    hk_perm = np.zeros((NHK_T, 96), np.int64)
    for j in range(NHK_T):
        for i, h in enumerate(heads_of(j)):
            hk_perm[j, i * 32:(i + 1) * 32] = h * 32 + np.arange(32)

    def to_cols(v):
        out = np.zeros((128, NHK_T), np.float32)
        for j in range(NHK_T):
            n = hk_rows(j)
            out[:n, j] = v[hk_perm[j, :n]]
        return out

    tb = to_cols(np.asarray(inputs["L0_attn_theta_base"], dtype=np.float32).reshape(-1))
    lb = to_cols(np.asarray(inputs["L0_attn_lambda_base"], dtype=np.float32).reshape(-1))
    eta = to_cols(np.asarray(inputs["L0_attn_viscosity_eta"], dtype=np.float32).reshape(-1))
    tw2 = np.asarray(inputs["L0_attn_theta_w2"], dtype=np.float32)
    tw2p = np.zeros((LORA, NHK_T * 96), np.float32)
    for j in range(NHK_T):
        n = hk_rows(j)
        tw2p[:, j * 96:j * 96 + n] = tw2[:, hk_perm[j, :n]]

    sel_dt_hk = np.zeros((NH, NHK_T * 96), np.float32)
    for j in range(NHK_T):
        for i, h in enumerate(heads_of(j)):
            sel_dt_hk[h, j * 96 + i * 32: j * 96 + (i + 1) * 32] = 1.0
    sel_dt_p = np.zeros((NH, 16 * 128), np.float32)
    for i in range(16):
        for hh in range(2):
            sel_dt_p[2 * i + hh, i * 128 + hh * 64: i * 128 + (hh + 1) * 64] = 1.0

    def dup_sel(base, sign):
        Pm = np.zeros((128, 96), np.float32)
        for b in range(3):
            Pm[base + np.arange(32), b * 32 + np.arange(32)] = sign
        return Pm
    dup_sels = np.concatenate([dup_sel(64, 1), dup_sel(64, -1), dup_sel(96, -1),
                               dup_sel(0, 1), dup_sel(0, -1), dup_sel(32, 1)], axis=1)

    rst = np.ones((128, TT), np.float32)
    rst[:, ::CH] = 0.0
    gm = (np.arange(64)[:, None] <= np.arange(64)[None, :]).astype(np.uint8)
    gmask = np.tile(gm, (1, 8))

    consts1 = np.zeros((128, 8), np.float32)
    consts1[:, 0] = PI / 2
    consts1[:, 1] = 1.0
    consts1[:, 2] = LN_EPS
    consts1[:, 3] = RMS_EPS
    consts1[:, 4] = -PI / 2

    return dict(
        Wp=Wp, conv_wt=conv_wt, conv_bt=conv_bt, tb=tb, lb=lb, eta=eta,
        tw1=np.asarray(inputs["L0_attn_theta_w1"], dtype=np.float32),
        tw2p=tw2p, sel_dt_hk=sel_dt_hk, sel_dt_p=sel_dt_p, dup_sels=dup_sels,
        ident=np.eye(128, dtype=np.float32), rst=rst, gmask=gmask,
        dt_bias_col=np.asarray(inputs["L0_attn_dt_bias"], dtype=np.float32).reshape(NH, 1),
        D_col=np.asarray(inputs["L0_attn_D"], dtype=np.float32).reshape(NH, 1),
        norm_w=np.tile(np.asarray(inputs["L0_attn_norm_w"], dtype=np.float32).reshape(1, DI), (128, 1)),
        D_rep=np.tile(np.asarray(inputs["L0_attn_D"], dtype=np.float32).reshape(1, NH), (128, 1)),
        ow=np.asarray(inputs["L0_attn_out_proj_w"], dtype=np.float32),
        rw=np.asarray(inputs["L0_ffn_router_w"], dtype=np.float32),
        rb=np.asarray(inputs["L0_ffn_router_b"], dtype=np.float32).reshape(1, NEXP),
        ew=np.asarray(inputs["L0_ffn_expert_w"], dtype=np.float32),
        eb=np.asarray(inputs["L0_ffn_expert_b"], dtype=np.float32),
        consts1=consts1,
        rb_rep=np.tile(np.asarray(inputs["L0_ffn_router_b"], dtype=np.float32).reshape(1, NEXP), (128, 1)),
        ones_row=np.ones((1, 128), np.float32),
    )


class Prog:
    ENGS = ("sync", "tensor", "vector", "scalar", "gpsimd")

    def __init__(self, nc):
        self.nc = nc
        self.ops = {e: [] for e in self.ENGS}
        self.sems = {}
        self.cnt = {}
        self.maxw = {}
        self.prewaited = {}
        self.dma_sems = set()
        self.pe_base = None
        self.last_mm = None
        self.last_mm_post = None

    def sem(self, name):
        self.cnt[name] = 0
        self.maxw[name] = 0
        return name

    def wait(self, eng, sem, val=None):
        v = self.cnt[sem] if val is None else val
        self.maxw[sem] = max(self.maxw[sem], v)
        if v > 0:
            self.ops[eng].append([None, ("wait", sem, v)])
        return v

    def do(self, eng, fn, inc=None, amt=1):
        entry = [fn, []]
        if inc is not None:
            self.cnt[inc] += amt
            entry[1].append((inc, amt))
        self.ops[eng].append(entry)
        return entry

    def mm(self, fn, base=0, inc=None, amt=1):
        if self.pe_base is not None and base != self.pe_base and self.last_mm is not None:
            if self.last_mm[1]:
                sem, v = self.last_mm_post
                self.ops["tensor"].append([None, ("wait", sem, v)])
            else:
                self.cnt["MMD"] += 1
                self.last_mm[1].append(("MMD", 1))
                self.last_mm_post = ("MMD", self.cnt["MMD"])
                self.ops["tensor"].append([None, ("wait", "MMD", self.cnt["MMD"])])
        self.pe_base = base
        e = self.do("tensor", fn, inc=inc, amt=amt)
        self.last_mm = e
        self.last_mm_post = (inc, self.cnt[inc]) if inc is not None else None
        return e

    def dma(self, eng, out, in_, inc=None, **kw):
        if inc is not None:
            key = (eng, inc)
            need = self.maxw[inc]
            if need > self.prewaited.get(key, 0):
                self.ops[eng].append([None, ("wait", inc, need)])
                self.prewaited[key] = need
        if inc is not None:
            self.dma_sems.add(inc)
        self.do(eng, lambda e: e.dma_start(out=out, in_=in_, **kw), inc=inc, amt=16)

    def build(self, block):
        for ename in self.ENGS:
            ops = self.ops[ename]

            def make(ops):
                def _(eng):
                    for entry in ops:
                        fn, incs = entry
                        if fn is None:
                            _, sem, v = incs
                            eng.wait_ge(self.sems[sem], v)
                            continue
                        inst = fn(eng)
                        for (sem, amt) in incs:
                            inst.then_inc(self.sems[sem], amt)
                return _
            getattr(block, ename)(make(ops))


def _slice3(t_, ks, j, col, n):
    return t_[ks, j, col:col + n]


def build_program(L=4096, TT=256, debug=0, stage=99, sub=-1):
    if sub < 0:
        yon, ston, gon, sfon, nchl, nq = True, True, True, True, 0, 3
    else:
        yon, ston, gon, sfon = bool(sub & 1), bool(sub & 2), bool(sub & 4), bool(sub & 8)
        nchl = (sub >> 4) & 7
        nq = ((sub >> 7) & 3) or 3
    nc = bass.Bass("TRN2", target_bir_lowering=False, detect_race_conditions=False)
    NT = L // TT
    NCH_T = TT // CH
    TC = TT // 128

    dx = nc.dram_tensor("x", [L, DM], F32, kind="ExternalInput").ap()
    d_out = nc.dram_tensor("out", [L, DM], F32, kind="ExternalOutput").ap()
    din = {}
    for name, shape, dt in [
        ("Wp", [DIP, DM], F32), ("conv_wt", [128, 17, 4], F32), ("conv_bt", [128, 17], F32),
        ("tb", [128, NHK_T], F32), ("lb", [128, NHK_T], F32), ("eta", [128, NHK_T], F32),
        ("tw1", [DM, LORA], F32), ("tw2p", [LORA, NHK_T * 96], F32),
        ("sel_dt_hk", [NH, NHK_T * 96], F32), ("sel_dt_p", [NH, 16 * 128], F32),
        ("dup_sels", [128, 576], F32), ("ident", [128, 128], F32),
        ("rst", [128, TT], F32), ("gmask", [64, 512], U8),
        ("dt_bias_col", [NH, 1], F32), ("D_col", [NH, 1], F32), ("norm_w", [128, DI], F32),
        ("D_rep", [128, NH], F32),
        ("ow", [DM, DI], F32), ("rw", [DM, NEXP], F32), ("rb", [1, NEXP], F32),
        ("ew", [NEXP, DM, DM], F32), ("eb", [NEXP, DM], F32), ("consts1", [128, 8], F32),
        ("rb_rep", [128, NEXP], F32), ("ones_row", [1, 128], F32),
    ]:
        din[name] = nc.dram_tensor(name, shape, dt, kind="ExternalInput").ap()

    skind = {"kind": "ExternalOutput"} if debug else {}
    s_xnf = nc.dram_tensor("s_xnf", [DM, L], BF16, **skind).ap()
    s_gf = nc.dram_tensor("s_gf", [DI, L], BF16, **skind).ap()
    s_part = nc.dram_tensor("s_part", [L, DM], BF16, **skind).ap()

    dbg = {}

    def dbg_out(name, shape, dt=F32):
        dbg[name] = nc.dram_tensor("dbg_" + name, shape, dt, kind="ExternalOutput").ap()
        return dbg[name]

    A = nc.alloc_sbuf_tensor
    PS = nc.alloc_psum_tensor
    passa_ctxs = []

    def AA(name, shape, dt):
        cm = nc.sbuf_tensor(name, shape, dt)
        h = cm.__enter__()
        passa_ctxs.append(cm)
        return h

    bankT = PS("bankT", [128, 1024], BF16)
    bank = [PS(f"bank{i}", [128, 512], F32) for i in range(6)]

    ident_sb = A("identb", [128, 128], BF16)
    ident32 = A("ident32", [128, 128], F32)
    gmask_sb = A("gmask_sb", [64, 512], U8)
    zero_sb = A("zero_sb", [128, 512], BF16)
    c1 = A("c1", [128, 8], F32)
    rmsc2 = A("rmsc2", [128, 4], F32)
    stat = A("stat", [128, 8], F32)
    rmsc = A("rmsc", [128, 4], F32)
    WT = [AA(f"WT{g}", [128, DIP], BF16) for g in range(8)]
    TW1 = [AA(f"TW1_{g}", [128, LORA], BF16) for g in range(8)]
    TW2 = AA("TW2", [LORA, NHK_T * 96], BF16)
    conv_w_sb = AA("conv_w", [128, 17, 4], F32)
    conv_b_sb = AA("conv_b", [128, 17], F32)
    tb_sb = AA("tb_sb", [128, NHK_T], F32)
    lb_sb = AA("lb_sb", [128, NHK_T], F32)
    eta_sb = AA("eta_sb", [128, NHK_T], F32)
    seldthk_sb = AA("seldthk", [NH, NHK_T * 96], F32)
    dupsels_sb = AA("dupsels", [128, 576], BF16)
    rst_sb = AA("rst_sb", [128, TT], F32)
    dtb_sb = AA("dtb_sb", [NH, 1], F32)
    D_sb = AA("D_sb", [NH, 1], F32)
    Drep_sb = AA("Drep_sb", [128, NH], F32)
    normw_sb = AA("normw_sb", [128, DI], BF16)

    x_sb = [AA("x0", [128, TC * DM], F32)]
    xn_f = AA("xn_f", [128, 8, TT], BF16)
    zsilu = AA("zsilu", [128, TC * DI], BF16)
    scr = zsilu.ap()[:, 0:DM]
    UN = AA("UN", [128, 18 * (TT + 3) + 17 * TT], BF16)
    zx_f = UN.ap()[:, 0:18 * (TT + 3)].rearrange("p (j c) -> p j c", c=TT + 3)
    cvo = UN.ap()[:, 18 * (TT + 3):].rearrange("p (j c) -> p j c", c=TT)
    y_tok = UN.ap()[:, 0:TC * DI]
    G_sbF = UN.ap()[:, TC * DI:TC * DI + 512]
    gfst = UN.ap()[:, TC * DI + 512:TC * DI + 1024]
    S_all = AA("S_all", [96, NHK_T, 2, 128], BF16)
    tmpS = AA("tmpS", [96, NHK_T * 64], F32)
    tailb = AA("tailb", [128, 51], BF16)
    dt_f = AA("dt_f", [NH, TT], F32)
    sp1 = AA("sp1", [NH, TT], F32)
    dt_tok = AA("dt_tok", [128, TC * NH], F32)
    ddt_tok = AA("ddt_tok", [128, TC * NH], F32)
    th1 = AA("th1", [128, TT], F32)
    lam1 = AA("lam1", [128, TT], F32)
    dtr1 = AA("dtr1", [128, TT], F32)
    r1 = AA("r1", [128, TT], F32)
    tc1 = AA("tc1", [128, TT], F32)
    lor1 = AA("lor1", [LORA, TT], BF16)
    manc = AA("manc", [128, NHK_T, 8], F32)
    emanc = AA("emanc", [128, NHK_T, 8], F32)
    er63 = AA("er63", [128, NHK_T, 4], F32)
    dmm = AA("dmm", [128, NHK_T, 4], F32)
    ewm = AA("ewm", [128, NHK_T, 8], F32)
    dcolR = AA("dcolR", [128, NHK_T, 4], F32)
    dcolI = AA("dcolI", [128, NHK_T, 4], F32)
    dcolIn = AA("dcolIn", [128, NHK_T, 4], F32)
    tmpd = AA("tmpd", [128, TT], F32)
    tmpe = AA("tmpe", [128, TT], F32)
    kbuf = AA("kbuf", [128, TT], I32)
    edp = AA("edp", [128, TT], BF16)
    edm = AA("edm", [128, TT], BF16)
    su = AA("su", [128, TT], BF16)
    cv = AA("cv", [128, TT], BF16)
    cc3 = AA("cc3", [128, 6, TT], BF16)
    u1 = AA("u1", [128, TT], BF16)
    u2 = AA("u2", [128, TT], BF16)
    v1 = AA("v1", [128, TT], BF16)
    v2 = AA("v2", [128, TT], BF16)
    tb1 = AA("tb1", [128, TT], BF16)
    tb2 = AA("tb2", [128, TT], BF16)
    CdR = AA("CdR", [128, NHK_T, TT], BF16)
    CdIn = AA("CdIn", [128, NHK_T, TT], BF16)
    BdR = AA("BdR", [128, NHK_T, TT], BF16)
    BdI = AA("BdI", [128, NHK_T, TT], BF16)
    CsR = AA("CsR", [128, NHK_T, TT], BF16)
    CsIn = AA("CsIn", [128, NHK_T, TT], BF16)
    BoPR = AA("BoPR", [128, NHK_T, 128], BF16)
    BoPIn = AA("BoPIn", [128, NHK_T, 128], BF16)
    FR = AA("FR", [128, TT], BF16)
    FI = AA("FI", [128, TT], BF16)
    xd_tok = AA("xd_tok", [128, 2 * DI], BF16)
    xn_tok = xd_tok.ap()[:, 0:TC * DM]
    FRt = AA("FRt", [128, 2 * NHK_T * 96], BF16)
    FIt = AA("FIt", [128, 2 * NHK_T * 96], BF16)

    P = Prog(nc)
    MMD = P.sem("MMD")
    VB = P.sem("VB")
    LD = P.sem("LD")
    LDG = P.sem("LDG")
    STG = P.sem("STG")
    XL = P.sem("XL")
    MM = P.sem("MM")
    VE = P.sem("VE")
    AC = P.sem("AC")
    PL = P.sem("PL")
    ST = P.sem("ST")

    for dst, src in [
        (conv_w_sb[:, :, :], din["conv_wt"][:, :, :]),
        (conv_b_sb[:, :], din["conv_bt"][:, :]),
        (tb_sb[:, :], din["tb"][:, :]), (lb_sb[:, :], din["lb"][:, :]),
        (eta_sb[:, :], din["eta"][:, :]),
        (dtb_sb[:, :], din["dt_bias_col"][:, :]), (D_sb[:, :], din["D_col"][:, :]),
        (rst_sb[:, :], din["rst"][:, :]), (gmask_sb[:, :], din["gmask"][:, :]),
        (c1[:, :], din["consts1"][:, :]), (ident32[:, :], din["ident"][:, :]),
        (seldthk_sb[:, :], din["sel_dt_hk"][:, :]),
    ]:
        P.dma("sync", dst, src, LD)
    for dst, src in [
        (dupsels_sb[:, :], din["dup_sels"][:, :]),
        (ident_sb[:, :], din["ident"][:, :]),
        (TW2[:, :], din["tw2p"][:, :]),
        (normw_sb[:, :], din["norm_w"][:, :]),
        (Drep_sb[:, :], din["D_rep"][:, :]),
    ]:
        P.dma("gpsimd", dst, src, LDG)
    for g in range(8):
        P.dma("gpsimd", TW1[g][:, :], din["tw1"][g * 128:(g + 1) * 128, :], LDG)
    P.do("vector", lambda e: e.memzero(zero_sb[:, :]), inc=VE)

    wstg = [UN.ap()[:, i * DM:(i + 1) * DM] for i in range(2)]
    ve_done_tile = {}
    for ft in range(34):
        rows = 128 if ft < 33 else 32
        if ft >= 2:
            P.wait("gpsimd", VE, ve_done_tile[ft - 2])
        P.dma("gpsimd", wstg[ft % 2][:rows, :], din["Wp"][ft * 128: ft * 128 + rows, :], LDG)
        P.wait("tensor", LDG)
        for g in range(8):
            slot = g % 8
            if ft > 0 or g >= 8:
                pass
            P.wait("tensor", VE)
            P.mm(lambda e, ft=ft, g=g, slot=slot, rows=rows: e.transpose(
                bankT[:, slot * 128: slot * 128 + rows],
                wstg[ft % 2][:rows, g * 128:(g + 1) * 128], ident_sb[:rows, :rows]), inc=MM)
            P.wait("vector", MM)
            P.do("vector", lambda e, ft=ft, g=g, slot=slot, rows=rows: e.tensor_copy(
                out=WT[g][:, ft * 128: ft * 128 + rows],
                in_=bankT[:, slot * 128: slot * 128 + rows]), inc=VE)
        ve_done_tile[ft] = P.cnt["VE"]

    if debug:
        o = dbg_out("WTd", [8, 128, DIP], BF16)
        P.wait("gpsimd", VE)
        for g in range(8):
            P.dma("gpsimd", o[g], WT[g][:, :], STG)

    def transpose_blocks(pairs, dtype_f32=False):
        bk = bank[5] if dtype_f32 else bankT
        idt = ident32 if dtype_f32 else ident_sb
        for i, (src, dst, rows) in enumerate(pairs):
            slot = i % (1 if dtype_f32 else 8)
            P.wait("tensor", VE)
            P.mm(lambda e, src=src, rows=rows, slot=slot, bk=bk, idt=idt: e.transpose(
                bk[:, slot * 128: slot * 128 + rows], src, idt[:rows, :rows]), inc=MM)
            P.wait("vector", MM)
            P.do("vector", lambda e, dst=dst, rows=rows, slot=slot, bk=bk: e.tensor_copy(
                out=dst, in_=bk[:, slot * 128: slot * 128 + rows]), inc=VE)

    P.do("vector", lambda e: e.memzero(S_all[:, :, :, :]))
    P.do("vector", lambda e: e.memzero(tmpS[:, :]))
    for t_ in [CdR, CdIn, BdR, BdI, CsR, CsIn, BoPR, BoPIn]:
        P.do("vector", lambda e, t_=t_: e.memzero(t_[:, :, :]))
    for t_ in [FRt, FIt, xd_tok]:
        P.do("vector", lambda e, t_=t_: e.memzero(t_[:, :]))
    P.do("vector", lambda e: e.memzero(tb1[:, :]), inc=VE)

    for it in range(NT):
        t0 = it * TT
        xb = x_sb[0]
        if debug:
            for en in ("vector", "scalar", "tensor", "gpsimd"):
                P.wait(en, ST)
        P.wait("sync", AC)
        P.dma("sync", xb.ap().rearrange("p (n d) -> p n d", n=TC),
              dx.rearrange("(n p) d -> p n d", p=128)[:, it * TC:(it + 1) * TC, :], XL)
        xlv = P.cnt["XL"]

        P.wait("scalar", XL, xlv)
        for tcc in range(TC):
            xsl = xb[:, tcc * DM:(tcc + 1) * DM]
            P.do("scalar", lambda e, xsl=xsl, tcc=tcc: e.activation(
                out=scr[:, :], in_=xsl, func=AF.Square,
                accum_out=stat[:, 4 + tcc:5 + tcc]), inc=AC)
            P.do("scalar", lambda e, xsl=xsl, tcc=tcc: e.activation(
                out=scr[:, :], in_=xsl, func=AF.Identity,
                accum_out=stat[:, tcc:tcc + 1]), inc=AC)
        P.wait("vector", AC)
        for tcc in range(TC):
            P.do("vector", lambda e, tcc=tcc: e.tensor_scalar(
                out=stat[:, 2 + tcc:3 + tcc], in0=stat[:, tcc:tcc + 1],
                scalar1=1.0 / DM, scalar2=None, op0=ALU.mult))
            P.do("vector", lambda e, tcc=tcc: e.tensor_scalar(
                out=stat[:, 6 + tcc:7 + tcc], in0=stat[:, 4 + tcc:5 + tcc],
                scalar1=1.0 / DM, scalar2=None, op0=ALU.mult))
            P.do("vector", lambda e, tcc=tcc: e.scalar_tensor_tensor(
                out=stat[:, 6 + tcc:7 + tcc], in0=stat[:, 2 + tcc:3 + tcc],
                scalar=stat[:, 2 + tcc:3 + tcc], in1=stat[:, 6 + tcc:7 + tcc],
                op0=ALU.mult, op1=ALU.subtract), inc=VE)
        P.wait("scalar", VE)
        for tcc in range(TC):
            P.do("scalar", lambda e, tcc=tcc: e.activation(
                out=stat[:, 6 + tcc:7 + tcc], in_=stat[:, 6 + tcc:7 + tcc],
                func=AF.Sqrt, scale=-1.0, bias=c1[:, 2:3]), inc=AC)
        P.wait("vector", AC)
        for tcc in range(TC):
            P.do("vector", lambda e, tcc=tcc: e.reciprocal(
                out=stat[:, 6 + tcc:7 + tcc], in_=stat[:, 6 + tcc:7 + tcc]), inc=VB)
        P.wait("vector", VB)
        for tcc in range(TC):
            P.do("vector", lambda e, tcc=tcc: e.tensor_scalar(
                out=xn_tok[:, tcc * DM:(tcc + 1) * DM], in0=xb[:, tcc * DM:(tcc + 1) * DM],
                scalar1=stat[:, 2 + tcc:3 + tcc], scalar2=stat[:, 6 + tcc:7 + tcc],
                op0=ALU.subtract, op1=ALU.mult), inc=VE)

        prs = []
        for g in range(8):
            for tcc in range(TC):
                prs.append((xn_tok[:, tcc * DM + g * 128: tcc * DM + (g + 1) * 128],
                            xn_f[:, g, tcc * 128:(tcc + 1) * 128], 128))
        transpose_blocks(prs)
        P.wait("sync", VE)
        P.dma("sync", s_xnf.rearrange("(g p) t -> p g t", p=128)[:, :, t0:t0 + TT],
              xn_f[:, :, :], ST)

        for tcc in range(TC):
            for q in range(4):
                bk = bank[q % 2]
                P.wait("tensor", AC)
                for g in range(8):
                    P.mm(lambda e, tcc=tcc, q=q, g=g, bk=bk: e.matmul(
                        bk[:, :], xn_f[:, g, tcc * 128:(tcc + 1) * 128],
                        WT[g][:, q * 512:(q + 1) * 512],
                        start=(g == 0), stop=(g == 7)), inc=MM)
                P.wait("scalar", MM)
                P.do("scalar", lambda e, tcc=tcc, q=q, bk=bk: e.activation(
                    out=zsilu[:, tcc * DI + q * 512: tcc * DI + (q + 1) * 512],
                    in_=bk[:, :], func=AF.Silu), inc=AC)

        P.wait("vector", AC)
        if it == 0:
            P.do("vector", lambda e: e.tensor_copy(
                out=zx_f[:, 0:17, 0:3],
                in_=zero_sb.ap()[:, 0:51].rearrange("p (a b) -> p a b", a=17)), inc=VE)
        else:
            P.do("vector", lambda e: e.tensor_copy(
                out=zx_f[:, 0:17, 0:3],
                in_=tailb.ap()[:, :].rearrange("p (a b) -> p a b", a=17)), inc=VE)
        for ft in range(18):
            fo = 2048 + ft * 128
            rows = 128 if ft < 17 else 32
            bk = bank[2 + ft % 2]
            P.wait("tensor", VE)
            for g in range(8):
                P.mm(lambda e, ft=ft, g=g, fo=fo, rows=rows, bk=bk: e.matmul(
                    bk[:rows, 0:TT], WT[g][:, fo:fo + rows], xn_f[:, g, :],
                    start=(g == 0), stop=(g == 7)), inc=MM)
            P.wait("vector", MM)
            if ft < 17:
                P.do("vector", lambda e, ft=ft, bk=bk: e.tensor_copy(
                    out=zx_f[:, ft, 3:3 + TT], in_=bk[:, 0:TT]), inc=VE)
            else:
                P.do("vector", lambda e, bk=bk: e.tensor_scalar(
                    out=sp1[:, :], in0=bk[0:NH, 0:TT], scalar1=dtb_sb[:, 0:1],
                    scalar2=None, op0=ALU.add), inc=VE)

        P.do("vector", lambda e: e.tensor_copy(
            out=tailb.ap()[:, :].rearrange("p (a b) -> p a b", a=17),
            in_=zx_f[:, 0:17, TT:TT + 3]), inc=VE)
        P.wait("scalar", VE)
        P.do("scalar", lambda e: e.activation(out=dt_f[:, :], in_=sp1[:, :], func=AF.Abs))
        P.do("scalar", lambda e: e.activation(out=dt_f[:, :], in_=dt_f[:, :],
                                              func=AF.Exp, scale=-1.0))
        P.do("scalar", lambda e: e.activation(out=dt_f[:, :], in_=dt_f[:, :],
                                              func=AF.Ln, bias=1.0), inc=AC)
        P.wait("vector", AC)
        P.do("vector", lambda e: e.tensor_scalar(out=sp1[:, :], in0=sp1[:, :],
                                                 scalar1=0.0, scalar2=None, op0=ALU.max))
        P.do("vector", lambda e: e.tensor_tensor(out=dt_f[:, :], in0=dt_f[:, :],
                                                 in1=sp1[:, :], op=ALU.add), inc=VE)

        P.wait("vector", VE)
        for j in range(17):
            P.do("vector", lambda e, j=j: e.tensor_scalar(
                out=cvo[:, j, :], in0=zx_f[:, j, 0:TT],
                scalar1=conv_w_sb[:, j, 0:1], scalar2=conv_b_sb[:, j:j + 1],
                op0=ALU.mult, op1=ALU.add))
            for k in range(1, 4):
                P.do("vector", lambda e, j=j, k=k: e.scalar_tensor_tensor(
                    out=cvo[:, j, :], in0=zx_f[:, j, k:k + TT],
                    scalar=conv_w_sb[:, j, k:k + 1], in1=cvo[:, j, :],
                    op0=ALU.mult, op1=ALU.add),
                    inc=(VE if (j == 16 and k == 3) else None))
        P.wait("scalar", VE)
        for j in range(17):
            P.do("scalar", lambda e, j=j: e.activation(
                out=cvo[:, j, :], in_=cvo[:, j, :], func=AF.Silu), inc=AC)

        prs = [(dt_f[0:NH, tcc * 128:(tcc + 1) * 128],
                dt_tok[:, tcc * NH:(tcc + 1) * NH], NH) for tcc in range(TC)]
        for i, (src, dst, rows) in enumerate(prs):
            P.wait("tensor", VE)
            P.mm(lambda e, src=src: e.transpose(
                bank[5][:, 0:NH], src, ident32[:NH, :NH]), inc=MM)
            P.wait("vector", MM)
            P.do("vector", lambda e, dst=dst: e.tensor_copy(
                out=dst, in_=bank[5][:, 0:NH]), inc=VE)
        P.wait("vector", VE)
        P.do("vector", lambda e: e.reciprocal(out=ddt_tok[:, :], in_=dt_tok[:, :]),
             inc=VB)
        P.wait("vector", VB)
        for tcc in range(TC):
            P.do("vector", lambda e, tcc=tcc: e.tensor_tensor(
                out=ddt_tok[:, tcc * NH:(tcc + 1) * NH],
                in0=ddt_tok[:, tcc * NH:(tcc + 1) * NH],
                in1=Drep_sb[:, :], op=ALU.mult), inc=VE)

        if debug and it == 0:
            P.wait("sync", VE)
            P.wait("sync", AC)
            ox = dbg_out("xn_tok", [128, TC * DM], BF16)
            P.dma("sync", ox[:, :], xn_tok[:, :], ST)
            ost = dbg_out("stat", [128, 8])
            P.dma("sync", ost[:, :], stat[:, :], ST)
            ozx = dbg_out("zx_f", [128, 17, TT + 3], BF16)
            P.dma("sync", ozx[:, :, :], zx_f[:, 0:17, :], ST)
            o1 = dbg_out("xn_f", [128, 8, TT], BF16)
            P.dma("sync", o1[:, :, :], xn_f[:, :, :], ST)
            o2 = dbg_out("zsilu", [128, TC * DI], BF16)
            P.dma("sync", o2[:, :], zsilu[:, :], ST)
            o3 = dbg_out("cvo", [128, 17, TT], BF16)
            P.dma("sync", o3[:, :, :], cvo[:, :, :], ST)
            o4 = dbg_out("dt_f", [NH, TT])
            P.dma("sync", o4[:, :], dt_f[:, :], ST)
            o5 = dbg_out("ddt_tok", [128, TC * NH])
            P.dma("sync", o5[:, :], ddt_tok[:, :], ST)
        if stage <= 1:
            continue

        P.wait("tensor", AC)
        for g in range(8):
            P.mm(lambda e, g=g: e.matmul(
                bank[4][0:LORA, 0:TT], TW1[g][:, :], xn_f[:, g, :],
                start=(g == 0), stop=(g == 7)), inc=MM)
        P.wait("scalar", MM)
        P.do("scalar", lambda e: e.activation(out=lor1[:, :], in_=bank[4][0:LORA, 0:TT],
                                              func=AF.Tanh), inc=AC)
        P.wait("vector", VE)
        P.wait("tensor", AC)
        for sI in range(6):
            P.wait("tensor", VE)
            P.mm(lambda e, sI=sI: e.matmul(
                bank[5][0:96, 0:TT], dupsels_sb[:, sI * 96:(sI + 1) * 96],
                cvo[:, 16, :], start=True, stop=True), inc=MM)
            P.wait("vector", MM)
            P.do("vector", lambda e, sI=sI: e.tensor_copy(
                out=cc3[:96, sI, :], in_=bank[5][0:96, 0:TT]), inc=VE)

        for j in range(NHK_T):
            R = hk_rows(j)
            bk = bank[j % 2]
            bk2 = bank[2 + j % 2]
            P.wait("tensor", VE)
            P.mm(lambda e, j=j, R=R, bk=bk: e.matmul(
                bk[0:R, 0:TT], TW2[:, j * 96: j * 96 + R], lor1[:, :],
                start=True, stop=True), inc=MM)
            P.mm(lambda e, j=j, R=R, bk2=bk2: e.matmul(
                bk2[0:R, 0:TT], seldthk_sb[:, j * 96: j * 96 + R], dt_f[:, :],
                start=True, stop=True), inc=MM)
            P.wait("vector", MM)
            P.do("vector", lambda e, R=R, bk=bk, j=j: e.tensor_scalar(
                out=th1[:R, :], in0=bk[0:R, 0:TT], scalar1=tb_sb[:R, j:j + 1],
                scalar2=c1[:R, 0:1], op0=ALU.add, op1=ALU.min))
            P.do("vector", lambda e, R=R, j=j: e.tensor_scalar(
                out=th1[:R, :], in0=th1[:R, :], scalar1=c1[:R, 4:5],
                scalar2=None, op0=ALU.max))
            P.do("vector", lambda e, R=R, bk2=bk2: e.tensor_copy(
                out=dtr1[:R, :], in_=bk2[0:R, 0:TT]), inc=VE)
            P.wait("scalar", VE)
            P.do("scalar", lambda e, R=R: e.activation(
                out=lam1[:R, :], in_=th1[:R, :], func=AF.Square), inc=AC)
            P.wait("vector", AC)
            P.do("vector", lambda e, R=R, j=j: e.tensor_scalar(
                out=lam1[:R, :], in0=lam1[:R, :], scalar1=eta_sb[:R, j:j + 1],
                scalar2=lb_sb[:R, j:j + 1], op0=ALU.mult, op1=ALU.add))
            P.do("vector", lambda e, R=R: e.scalar_tensor_tensor(
                out=lam1[:R, :], in0=lam1[:R, :], scalar=-1.0,
                in1=dtr1[:R, :], op0=ALU.mult, op1=ALU.mult))
            P.do("vector", lambda e, R=R: e.tensor_tensor(
                out=th1[:R, :], in0=th1[:R, :], in1=dtr1[:R, :], op=ALU.mult))
            P.do("vector", lambda e, R=R: e.tensor_tensor_scan(
                out=r1[:R, :], data0=rst_sb[:R, :], data1=lam1[:R, :],
                initial=0.0, op0=ALU.mult, op1=ALU.add), inc=VB)
            P.do("vector", lambda e, R=R: e.tensor_tensor_scan(
                out=tc1[:R, :], data0=rst_sb[:R, :], data1=th1[:R, :],
                initial=0.0, op0=ALU.mult, op1=ALU.add), inc=VB)
            P.wait("vector", VB)
            ra = r1.ap()[:, :].rearrange("p (c q) -> p c q", q=32)
            rt = ra.tensor
            ro = ra.offset
            PS_ = TT

            def colap(off, cstride, n, rep=1):
                return bass.AP(rt, ro + off, [[PS_, 128], [cstride, n], [0, rep]])
            P.do("vector", lambda e, R=R, j=j, a0=colap(0, 32, 8), b31=colap(31, 32, 8):
                 e.tensor_tensor(out=manc[:R, j, :].rearrange("p (n o) -> p n o", o=1),
                                 in0=a0[:R], in1=b31[:R], op=ALU.add), inc=VB)
            P.wait("vector", VB)
            mb = bass.AP(manc.ap().tensor, manc.ap().offset + j * 8,
                         [[NHK_T * 8, 128], [1, 8], [0, 32]])
            P.do("vector", lambda e, R=R, mb=mb, ra=ra: e.scalar_tensor_tensor(
                out=tmpd[:R, :].rearrange("p (c q) -> p c q", q=32), in0=mb[:R],
                scalar=-0.5, in1=ra[:R], op0=ALU.mult, op1=ALU.add), inc=VE)
            P.wait("scalar", VE)
            P.do("scalar", lambda e, R=R: e.activation(
                out=edp[:R, :], in_=tmpd[:R, :], func=AF.Exp))
            P.do("scalar", lambda e, R=R: e.activation(
                out=edm[:R, :], in_=tmpd[:R, :], func=AF.Exp, scale=-1.0))
            P.do("scalar", lambda e, R=R, j=j: e.activation(
                out=emanc[:R, j, :], in_=manc[:R, j, :], func=AF.Exp,
                scale=0.5), inc=AC)
            P.wait("vector", AC)
            P.do("vector", lambda e, R=R, j=j, b63=colap(63, 64, 4): e.tensor_copy(
                out=er63[:R, j, :].rearrange("p (n o) -> p n o", o=1), in_=b63[:R]))
            mw = bass.AP(manc.ap().tensor, manc.ap().offset + j * 8,
                         [[NHK_T * 8, 128], [1, 8]])
            P.do("vector", lambda e, R=R, j=j, b63w=colap(63, 64, 4, 2), mw=mw:
                 e.scalar_tensor_tensor(
                     out=ewm[:R, j, :].rearrange("p (c q) -> p c q", q=2),
                     in0=mw[:R].rearrange("p (c q) -> p c q", q=2),
                     scalar=-0.5, in1=b63w[:R], op0=ALU.mult, op1=ALU.add))
            m0c = bass.AP(manc.ap().tensor, manc.ap().offset + j * 8,
                          [[NHK_T * 8, 128], [2, 4], [0, 1]])
            m1c = bass.AP(manc.ap().tensor, manc.ap().offset + j * 8 + 1,
                          [[NHK_T * 8, 128], [2, 4], [0, 1]])
            P.do("vector", lambda e, R=R, j=j, m0c=m0c, m1c=m1c: e.tensor_tensor(
                out=dmm[:R, j, :].rearrange("p (n o) -> p n o", o=1),
                in0=m1c[:R], in1=m0c[:R], op=ALU.subtract), inc=VE)
            P.wait("scalar", VE)
            P.do("scalar", lambda e, R=R, j=j: e.activation(
                out=er63[:R, j, :], in_=er63[:R, j, :], func=AF.Exp))
            P.do("scalar", lambda e, R=R, j=j: e.activation(
                out=ewm[:R, j, :], in_=ewm[:R, j, :], func=AF.Exp))
            P.do("scalar", lambda e, R=R, j=j: e.activation(
                out=dmm[:R, j, :], in_=dmm[:R, j, :], func=AF.Exp,
                scale=0.5), inc=AC)
            P.do("vector", lambda e, R=R: e.tensor_scalar(
                out=tmpd[:R, :], in0=tc1[:R, :], scalar1=1.0 / (2 * PI),
                scalar2=None, op0=ALU.mult))
            P.do("vector", lambda e, R=R: e.tensor_copy(out=kbuf[:R, :], in_=tmpd[:R, :]))
            P.do("vector", lambda e, R=R: e.tensor_copy(out=tmpd[:R, :], in_=kbuf[:R, :]))
            P.do("vector", lambda e, R=R: e.scalar_tensor_tensor(
                out=tc1[:R, :], in0=tmpd[:R, :], scalar=-2 * PI,
                in1=tc1[:R, :], op0=ALU.mult, op1=ALU.add))
            P.do("vector", lambda e, R=R: e.tensor_scalar(
                out=tmpd[:R, :], in0=tc1[:R, :], scalar1=-PI, scalar2=PI,
                op0=ALU.max, op1=ALU.min))
            P.do("vector", lambda e, R=R: e.tensor_scalar(
                out=tmpe[:R, :], in0=tc1[:R, :], scalar1=PI / 2, scalar2=None,
                op0=ALU.is_gt))
            P.do("vector", lambda e, R=R: e.scalar_tensor_tensor(
                out=tmpe[:R, :], in0=tmpe[:R, :], scalar=-2 * PI,
                in1=tc1[:R, :], op0=ALU.mult, op1=ALU.add), inc=VE)
            P.wait("scalar", VE)
            P.do("scalar", lambda e, R=R: e.activation(
                out=su[:R, :], in_=tmpd[:R, :], func=AF.Sin))
            P.do("scalar", lambda e, R=R: e.activation(
                out=cv[:R, :], in_=tmpe[:R, :], func=AF.Sin, bias=c1[:R, 0:1]), inc=AC)
            if debug >= 3 and it == 0 and j == 0:
                P.wait("sync", VE)
                P.wait("sync", AC)
                for nm, t_ in [("th1", th1), ("lam1", lam1), ("dtr1", dtr1),
                               ("r1", r1), ("tc1", tc1), ("tmpd", tmpd), ("tmpe", tmpe)]:
                    od = dbg_out("j0_" + nm, [96, TT])
                    P.dma("sync", od[:, :], t_[:96, :], ST)
                for nm, t_ in [("edp", edp), ("edm", edm), ("su", su), ("cv", cv)]:
                    od = dbg_out("j0_" + nm, [96, TT], BF16)
                    P.dma("sync", od[:, :], t_[:96, :], ST)
                od = dbg_out("j0_manc", [96, 8])
                P.dma("sync", od[:, :], manc[:96, 0, :], ST)
                stv_ = P.cnt["ST"]
                for en in ("vector", "scalar", "tensor"):
                    P.wait(en, ST, stv_)
            cv63 = bass.AP(cv.ap().tensor, cv.ap().offset + 63,
                           [[TT, 128], [64, 4], [0, 1]])
            su63 = bass.AP(su.ap().tensor, su.ap().offset + 63,
                           [[TT, 128], [64, 4], [0, 1]])
            P.wait("vector", AC)
            P.do("vector", lambda e, R=R, j=j, cv63=cv63: e.tensor_tensor(
                out=dcolR[:R, j, :].rearrange("p (n o) -> p n o", o=1),
                in0=er63[:R, j, :].rearrange("p (n o) -> p n o", o=1), in1=cv63[:R],
                op=ALU.mult))
            P.do("vector", lambda e, R=R, j=j, su63=su63: e.tensor_tensor(
                out=dcolI[:R, j, :].rearrange("p (n o) -> p n o", o=1),
                in0=er63[:R, j, :].rearrange("p (n o) -> p n o", o=1), in1=su63[:R],
                op=ALU.mult))
            P.do("vector", lambda e, R=R, j=j, su63=su63: e.scalar_tensor_tensor(
                out=dcolIn[:R, j, :].rearrange("p (n o) -> p n o", o=1),
                in0=er63[:R, j, :].rearrange("p (n o) -> p n o", o=1),
                scalar=-1.0, in1=su63[:R], op0=ALU.mult, op1=ALU.mult))

            def TTm(out, i0, i1, inc=None):
                P.do("vector", lambda e, out=out, i0=i0, i1=i1: e.tensor_tensor(
                    out=out, in0=i0, in1=i1, op=ALU.mult), inc=inc)

            def TTa(out, i0, i1, inc=None, op=ALU.add):
                P.do("vector", lambda e, out=out, i0=i0, i1=i1, op=op: e.tensor_tensor(
                    out=out, in0=i0, in1=i1, op=op), inc=inc)

            cvj = cv[:R, :]
            suj = su[:R, :]
            TTm(u1[:R, :], cc3[:R, 0, :], cvj)
            TTm(tb1[:R, :], cc3[:R, 2, :], suj)
            TTa(u1[:R, :], u1[:R, :], tb1[:R, :])
            TTm(u2[:R, :], cc3[:R, 1, :], suj)
            TTm(tb1[:R, :], cc3[:R, 2, :], cvj)
            TTa(u2[:R, :], u2[:R, :], tb1[:R, :])
            TTm(v1[:R, :], cc3[:R, 3, :], cvj)
            TTm(tb1[:R, :], cc3[:R, 5, :], suj)
            TTa(v1[:R, :], v1[:R, :], tb1[:R, :])
            TTm(v2[:R, :], cc3[:R, 5, :], cvj)
            TTm(tb1[:R, :], cc3[:R, 4, :], suj)
            TTa(v2[:R, :], v2[:R, :], tb1[:R, :])
            TTm(CdR[:R, j, :], u1[:R, :], edp[:R, :])
            TTm(CdIn[:R, j, :], u2[:R, :], edp[:R, :])
            TTm(BdR[:R, j, :], v1[:R, :], edm[:R, :])
            TTm(BdI[:R, j, :], v2[:R, :], edm[:R, :])
            emb = bass.AP(emanc.ap().tensor, emanc.ap().offset + j * 8,
                          [[NHK_T * 8, 128], [1, 8], [0, 32]])
            P.do("vector", lambda e, R=R, j=j, emb=emb: e.tensor_tensor(
                out=CsR[:R, j, :].rearrange("p (c q) -> p c q", q=32),
                in0=CdR[:R, j, :].rearrange("p (c q) -> p c q", q=32),
                in1=emb[:R], op=ALU.mult))
            P.do("vector", lambda e, R=R, j=j, emb=emb: e.tensor_tensor(
                out=CsIn[:R, j, :].rearrange("p (c q) -> p c q", q=32),
                in0=CdIn[:R, j, :].rearrange("p (c q) -> p c q", q=32),
                in1=emb[:R], op=ALU.mult))
            dmb = bass.AP(dmm.ap().tensor, dmm.ap().offset + j * 4,
                          [[NHK_T * 4, 128], [1, 4], [0, 32]])

            def sub0_ap(t_, jj):
                a = t_.ap()
                return bass.AP(a.tensor, a.offset + jj * TT, [[NHK_T * TT, 128], [64, 4], [1, 32]])
            P.do("vector", lambda e, R=R, j=j, dmb=dmb: e.tensor_tensor(
                out=BoPR[:R, j, :].rearrange("p (c q) -> p c q", q=32),
                in0=sub0_ap(BdR, j)[:R], in1=dmb[:R], op=ALU.mult))
            P.do("vector", lambda e, R=R, j=j, dmb=dmb: e.tensor_tensor(
                out=BoPIn[:R, j, :].rearrange("p (c q) -> p c q", q=32),
                in0=sub0_ap(BdI, j)[:R], in1=dmb[:R], op=ALU.mult))
            cv63b = bass.AP(cv.ap().tensor, cv.ap().offset + 63,
                            [[TT, 128], [64, 4], [0, 64]])
            su63b = bass.AP(su.ap().tensor, su.ap().offset + 63,
                            [[TT, 128], [64, 4], [0, 64]])
            ewmb = bass.AP(ewm.ap().tensor, ewm.ap().offset + j * 8,
                           [[NHK_T * 8, 128], [1, 8], [0, 32]])
            BdRj = BdR[:R, j, :].rearrange("p (c q) -> p c q", q=64)
            BdIj = BdI[:R, j, :].rearrange("p (c q) -> p c q", q=64)
            q64 = lambda t_: t_.ap()[:R, :].rearrange("p (c q) -> p c q", q=64)
            q32 = lambda t_: t_.ap()[:R, :].rearrange("p (c q) -> p c q", q=32)
            TTm(q64(tb1), BdRj, cv63b[:R])
            TTm(q64(tb2), BdIj, su63b[:R])
            TTm(q64(u1), BdIj, cv63b[:R])
            P.do("vector", lambda e, o=q64(u2), i0=BdRj, i1=su63b[:R]: e.tensor_tensor(
                out=o, in0=i0, in1=i1, op=ALU.mult), inc=VB)
            P.wait("vector", VB)
            TTa(tb1[:R, :], tb1[:R, :], tb2[:R, :], op=ALU.subtract)
            TTa(u1[:R, :], u1[:R, :], u2[:R, :])
            _o1, _i1, _w1 = q32(FR), q32(tb1), ewmb[:R]
            P.do("vector", lambda e, o=_o1, i=_i1, w=_w1: e.tensor_tensor(
                out=o, in0=i, in1=w, op=ALU.mult))
            _o2, _i2 = q32(FI), q32(u1)
            P.do("vector", lambda e, o=_o2, i=_i2, w=_w1: e.tensor_tensor(
                out=o, in0=i, in1=w, op=ALU.mult), inc=VE)
            prsf = []
            for tcc in range(TC):
                prsf.append((FR[:R, tcc * 128:(tcc + 1) * 128],
                             FRt[:, tcc * NHK_T * 96 + j * 96: tcc * NHK_T * 96 + j * 96 + R], R))
                prsf.append((FI[:R, tcc * 128:(tcc + 1) * 128],
                             FIt[:, tcc * NHK_T * 96 + j * 96: tcc * NHK_T * 96 + j * 96 + R], R))
            transpose_blocks(prsf)

        prsx = []
        for i in range(16):
            for tcc in range(TC):
                prsx.append((cvo[:, i, tcc * 128:(tcc + 1) * 128],
                             xd_tok[:, tcc * DI + i * 128: tcc * DI + (i + 1) * 128], 128))
        transpose_blocks(prsx)
        for tcc in range(TC):
            dtb_ap = bass.AP(dt_tok.ap().tensor, dt_tok.ap().offset + tcc * NH,
                             [[TC * NH, 128], [1, NH], [0, HD]])
            P.do("vector", lambda e, tcc=tcc, dtb_ap=dtb_ap: e.tensor_tensor(
                out=xd_tok.ap()[:, tcc * DI:(tcc + 1) * DI].rearrange("p (h q) -> p h q", q=HD),
                in0=xd_tok.ap()[:, tcc * DI:(tcc + 1) * DI].rearrange("p (h q) -> p h q", q=HD),
                in1=dtb_ap, op=ALU.mult), inc=VE)

        if debug >= 2 and it == 0:
            P.wait("sync", VE)
            P.wait("sync", AC)
            for nm, t_ in [("CdR", CdR), ("CdIn", CdIn), ("BdR", BdR), ("BdI", BdI),
                           ("CsR", CsR)]:
                od = dbg_out(nm, [128, NHK_T, TT], BF16)
                P.dma("sync", od[:, :, :], t_[:, :, :], ST)
            for nm, t_ in [("BoPR", BoPR), ("BoPIn", BoPIn)]:
                od = dbg_out(nm, [128, NHK_T, 128], BF16)
                P.dma("sync", od[:, :, :], t_[:, :, :], ST)
            od = dbg_out("xd_tok", [128, 2 * DI], BF16)
            P.dma("sync", od[:, :], xd_tok[:, :], ST)
            od = dbg_out("cc3", [96, 6, TT], BF16)
            P.dma("sync", od[:, :, :], cc3[:96, :, :], ST)
            od = dbg_out("FRt", [128, 2 * NHK_T * 96], BF16)
            P.dma("sync", od[:, :], FRt[:, :], ST)
            od = dbg_out("FIt", [128, 2 * NHK_T * 96], BF16)
            P.dma("sync", od[:, :], FIt[:, :], ST)
        if stage <= 2:
            continue

        P.wait("vector", VE)
        for tcc in range(TC):
            ddt_ap = bass.AP(ddt_tok.ap().tensor, ddt_tok.ap().offset + tcc * NH,
                             [[TC * NH, 128], [1, NH], [0, HD]])
            P.do("vector", lambda e, tcc=tcc, ddt_ap=ddt_ap: e.tensor_tensor(
                out=y_tok[:, tcc * DI:(tcc + 1) * DI].rearrange("p (h q) -> p h q", q=HD),
                in0=xd_tok.ap()[:, tcc * DI:(tcc + 1) * DI].rearrange("p (h q) -> p h q", q=HD),
                in1=ddt_ap, op=ALU.mult), inc=VE)

        for c in range(min(NCH_T, nchl) if nchl else NCH_T):
            cg = it * NCH_T + c
            pc = cg % 2
            tcc = (c * CH) // 128
            rowbase = (c * CH) % 128
            c64 = c * CH
            for o8 in range(4):
                P.wait("tensor", VE)
                hhg = sorted(range(8), key=lambda hh: (
                    ((o8 * 8 + hh) % 3) * 32 != rowbase, (o8 * 8 + hh) % 3))
                for idx, hh in enumerate(hhg):
                    h = o8 * 8 + hh
                    j, iw = h // 3, (h % 3) * 32
                    ks = slice(iw, iw + 32)
                    _quads = [
                        (0, 0, BdR, BdI, c64, c64),
                        (32, 32, BdR, BdI, c64 + 32, c64 + 32),
                        (0, 32, BoPR, BoPIn, c * 32, c64 + 32),
                    ]
                    _quads = _quads[:nq]
                    for (so, to, lR_, lI_, lc, rc) in _quads:
                        _go = bank[0][so:so + 32, hh * 64 + to: hh * 64 + to + 32]
                        if sub >= 0 and (sub & 256):
                            _lr = _li = _rr = _ri = zero_sb[iw:iw + 32, 0:32]
                        else:
                            _lr = _slice3(lR_, ks, j, lc, 32)
                            _li = _slice3(lI_, ks, j, lc, 32)
                            _rr = _slice3(CdR, ks, j, rc, 32)
                            _ri = _slice3(CdIn, ks, j, rc, 32)
                        _last = (idx == 7 and (so, to) == _quads[-1][:2])
                        P.mm(lambda e, o=_go, l=_lr, r=_rr: e.matmul(
                            o, l, r, start=True, stop=False), base=iw)
                        P.mm(lambda e, o=_go, l=_li, r=_ri: e.matmul(
                            o, l, r, start=False, stop=True), base=iw,
                            inc=MM if _last else None)
                P.wait("vector", MM)
                G_sb = G_sbF[rowbase:rowbase + 64, :]
                P.do("vector", lambda e, G_sb=G_sb: e.tensor_copy(
                    out=G_sb, in_=zero_sb[0:64, :]))
                P.do("vector", lambda e, G_sb=G_sb: e.copy_predicated(
                    out=G_sb, mask=gmask_sb[:, :], data=bank[0][0:64, :]), inc=VE)
                P.wait("tensor", VE)
                for hh in range(8 if yon else 0):
                    h = o8 * 8 + hh
                    _yo = bank[1][0:64, hh * 64:(hh + 1) * 64]
                    _gl = G_sb[:, hh * 64:(hh + 1) * 64]
                    _xr = xd_tok.ap()[rowbase:rowbase + 64,
                                      tcc * DI + h * 64: tcc * DI + (h + 1) * 64]
                    P.mm(lambda e, o=_yo, l=_gl, r=_xr: e.matmul(
                        o, l, r, start=True, stop=True), base=rowbase)
                hhy = sorted(range(8), key=lambda hh: (
                    ((o8 * 8 + hh) % 3) * 32 == rowbase, (o8 * 8 + hh) % 3))
                for idx, hh in enumerate(hhy if yon else []):
                    h = o8 * 8 + hh
                    j, iw = h // 3, (h % 3) * 32
                    ks = slice(iw, iw + 32)
                    _cs = _slice3(CsR, ks, j, c64, CH)
                    _ci = _slice3(CsIn, ks, j, c64, CH)
                    _sr = S_all[ks, j, pc, 0:64]
                    _si = S_all[ks, j, pc, 64:128]
                    _yo2 = bank[3][0:64, hh * 64:(hh + 1) * 64]
                    P.mm(lambda e, o=_yo2, l=_cs, r=_sr: e.matmul(
                        o, l, r, start=True, stop=False), base=iw)
                    P.mm(lambda e, o=_yo2, l=_ci, r=_si: e.matmul(
                        o, l, r, start=False, stop=True), base=iw,
                        inc=MM if idx == 7 else None)
                P.wait("vector", MM)
                yslice = y_tok[rowbase:rowbase + 64, tcc * DI + o8 * 512: tcc * DI + (o8 + 1) * 512]
                P.do("vector", lambda e, yslice=yslice: e.tensor_tensor(
                    out=yslice, in0=bank[1][0:64, :], in1=yslice, op=ALU.add))
                P.do("vector", lambda e, yslice=yslice: e.tensor_tensor(
                    out=yslice, in0=bank[3][0:64, :], in1=yslice, op=ALU.add), inc=VE)
            for j in range(NHK_T if ston else 0):
                R = hk_rows(j)
                nh3 = len(heads_of(j))
                P.wait("tensor", VE)
                for i in range(nh3):
                    h = 3 * j + i
                    rw = i * 32
                    _fcol = tcc * NHK_T * 96 + j * 96 + rw
                    _xr = xd_tok.ap()[rowbase:rowbase + 64,
                                      tcc * DI + h * 64: tcc * DI + (h + 1) * 64]
                    _fr = FRt[rowbase:rowbase + 64, _fcol:_fcol + 32]
                    _fi = FIt[rowbase:rowbase + 64, _fcol:_fcol + 32]
                    _or_ = bank[2][rw:rw + 32, 0:64]
                    _oi = bank[2][rw:rw + 32, 64:128]
                    P.mm(lambda e, o=_or_, l=_fr, r=_xr: e.matmul(
                        o, l, r, start=True, stop=True), base=rowbase)
                    P.mm(lambda e, o=_oi, l=_fi, r=_xr: e.matmul(
                        o, l, r, start=True, stop=True), base=rowbase,
                        inc=MM if i == nh3 - 1 else None)
                P.wait("vector", MM)
                tS = tmpS[0:R, j * 64:(j + 1) * 64]
                dR_ = dcolR[:R, j, c:c + 1]
                dI_ = dcolI[:R, j, c:c + 1]
                dIn_ = dcolIn[:R, j, c:c + 1]
                Spc = S_all[0:R, j, pc, :]
                Snx = S_all[0:R, j, 1 - pc, :]
                P.do("vector", lambda e, tS=tS, Spc=Spc, dR_=dR_: e.tensor_scalar(
                    out=tS, in0=Spc[:, 0:64], scalar1=dR_, scalar2=None, op0=ALU.mult))
                P.do("vector", lambda e, tS=tS, Spc=Spc, dIn_=dIn_: e.scalar_tensor_tensor(
                    out=tS, in0=Spc[:, 64:128], scalar=dIn_, in1=tS,
                    op0=ALU.mult, op1=ALU.add))
                _u0 = bank[2][0:R, 0:64]
                P.do("vector", lambda e, tS=tS, Snx=Snx, u=_u0: e.tensor_tensor(
                    out=Snx[:, 0:64], in0=tS, in1=u, op=ALU.add))
                P.do("vector", lambda e, tS=tS, Spc=Spc, dI_=dI_: e.tensor_scalar(
                    out=tS, in0=Spc[:, 0:64], scalar1=dI_, scalar2=None, op0=ALU.mult))
                P.do("vector", lambda e, tS=tS, Spc=Spc, dR_=dR_: e.scalar_tensor_tensor(
                    out=tS, in0=Spc[:, 64:128], scalar=dR_, in1=tS,
                    op0=ALU.mult, op1=ALU.add))
                _u1 = bank[2][0:R, 64:128]
                P.do("vector", lambda e, tS=tS, Snx=Snx, u=_u1: e.tensor_tensor(
                    out=Snx[:, 64:128], in0=tS, in1=u, op=ALU.add), inc=VE)

        if debug >= 3 and it == 0:
            P.wait("sync", VE)
            od = dbg_out("y_tok", [128, TC * DI], BF16)
            P.dma("sync", od[:, :], y_tok[:, :], ST)
            stv_ = P.cnt["ST"]
            for en in ("vector", "scalar", "tensor"):
                P.wait(en, ST, stv_)

        P.wait("vector", VE)
        for tcc in range(TC if gon else 0):
            P.do("vector", lambda e, tcc=tcc: e.tensor_tensor(
                out=y_tok[:, tcc * DI:(tcc + 1) * DI], in0=y_tok[:, tcc * DI:(tcc + 1) * DI],
                in1=zsilu[:, tcc * DI:(tcc + 1) * DI], op=ALU.mult), inc=VE)
        if debug >= 3 and it == 0:
            P.wait("sync", VE)
            od = dbg_out("y_gate", [128, TC * DI], BF16)
            P.dma("sync", od[:, :], y_tok[:, :], ST)
            stv_ = P.cnt["ST"]
            for en in ("vector", "scalar", "tensor"):
                P.wait(en, ST, stv_)
        P.wait("scalar", VE)
        for tcc in range(TC):
            P.do("scalar", lambda e, tcc=tcc: e.activation(
                out=scr[:, :], in_=y_tok[:, tcc * DI: tcc * DI + DM], func=AF.Square,
                accum_out=rmsc[:, 2 * tcc:2 * tcc + 1]))
            P.do("scalar", lambda e, tcc=tcc: e.activation(
                out=scr[:, :], in_=y_tok[:, tcc * DI + DM:(tcc + 1) * DI], func=AF.Square,
                accum_out=rmsc[:, 2 * tcc + 1:2 * tcc + 2]), inc=AC)
        P.wait("vector", AC)
        for tcc in range(TC):
            P.do("vector", lambda e, tcc=tcc: e.tensor_tensor(
                out=rmsc[:, 2 * tcc:2 * tcc + 1], in0=rmsc[:, 2 * tcc:2 * tcc + 1],
                in1=rmsc[:, 2 * tcc + 1:2 * tcc + 2], op=ALU.add), inc=VE)
        if debug >= 3 and it == 0:
            P.wait("sync", VE)
            od = dbg_out("rmsc_d", [128, 4])
            P.dma("sync", od[:, :], rmsc[:, :], ST)
            stv_ = P.cnt["ST"]
            for en in ("vector", "scalar", "tensor"):
                P.wait(en, ST, stv_)
        P.wait("scalar", VE)
        for tcc in range(TC):
            P.do("scalar", lambda e, tcc=tcc: e.activation(
                out=rmsc[:, 2 * tcc:2 * tcc + 1], in_=rmsc[:, 2 * tcc:2 * tcc + 1],
                func=AF.Sqrt, scale=1.0 / DI, bias=c1[:, 3:4]), inc=AC)
        P.wait("vector", AC)
        for tcc in range(TC):
            P.do("vector", lambda e, tcc=tcc: e.reciprocal(
                out=rmsc[:, 2 * tcc:2 * tcc + 1], in_=rmsc[:, 2 * tcc:2 * tcc + 1]),
                inc=VB)
        P.wait("vector", VB)
        for tcc in range(TC):
            P.do("vector", lambda e, tcc=tcc: e.tensor_scalar(
                out=y_tok[:, tcc * DI:(tcc + 1) * DI], in0=y_tok[:, tcc * DI:(tcc + 1) * DI],
                scalar1=rmsc[:, 2 * tcc:2 * tcc + 1], scalar2=None, op0=ALU.mult))
            P.do("vector", lambda e, tcc=tcc: e.tensor_tensor(
                out=y_tok[:, tcc * DI:(tcc + 1) * DI], in0=y_tok[:, tcc * DI:(tcc + 1) * DI],
                in1=normw_sb[:, :], op=ALU.mult), inc=VE)
        if debug >= 3 and it == 0:
            P.wait("sync", VE)
            od = dbg_out("g_tok", [128, TC * DI], BF16)
            P.dma("sync", od[:, :], y_tok[:, :], ST)
            stv_ = P.cnt["ST"]
            for en in ("vector", "scalar", "tensor"):
                P.wait(en, ST, stv_)
        for tcc in range(TC if sfon else 0):
            for grp in range(4):
                P.wait("vector", ST)
                prsg = []
                for b in range(4):
                    di = grp * 4 + b
                    prsg.append((y_tok[:, tcc * DI + di * 128: tcc * DI + (di + 1) * 128],
                                 gfst[:, b * 128:(b + 1) * 128], 128))
                transpose_blocks(prsg)
                P.wait("sync", VE)
                P.dma("sync",
                      s_gf.rearrange("(g p) t -> p g t", p=128)
                      [:, grp * 4:(grp + 1) * 4, t0 + tcc * 128: t0 + tcc * 128 + 128],
                      gfst[:, :].rearrange("p (g t) -> p g t", t=128), ST)



    for en in ("tensor", "vector", "scalar", "gpsimd", "sync"):
        P.wait(en, VE)
        P.wait(en, AC)
        P.wait(en, MM)
        P.wait(en, ST)
    for cm in reversed(passa_ctxs):
        cm.__exit__(None, None, None)

    if stage >= 4:
        TT1 = 512
        NT1 = L // TT1
        OWT = [A(f"OWT{g2}", [128, DM], BF16) for g2 in range(16)]
        wstgB = [A(f"wstgB{i}", [128, DI], BF16) for i in range(2)]
        gf1 = A("gf1", [128, 16, TT1], BF16)
        xb1 = A("xb1", [128, (TT1 // 128) * DM], F32)
        pstage = A("pstage", [128, (TT1 // 128) * DM], BF16)
        ones1 = A("ones1", [1, 128], BF16)
        P.dma("gpsimd", ones1[:, :], din["ones_row"][:, :], LDG)
        for dot in range(8):
            if dot >= 2:
                P.wait("gpsimd", VE)
            P.dma("gpsimd", wstgB[dot % 2][:, :], din["ow"][dot * 128:(dot + 1) * 128, :], LDG)
            P.wait("tensor", LDG)
            for dit in range(16):
                slot = dit % 8
                P.wait("tensor", VE)
                P.mm(lambda e, dot=dot, dit=dit, slot=slot: e.transpose(
                    bankT[:, slot * 128:(slot + 1) * 128],
                    wstgB[dot % 2][:, dit * 128:(dit + 1) * 128], ident_sb[:, :]), inc=MM)
                P.wait("vector", MM)
                P.do("vector", lambda e, dot=dot, dit=dit, slot=slot: e.tensor_copy(
                    out=OWT[dit][:, dot * 128:(dot + 1) * 128],
                    in_=bankT[:, slot * 128:(slot + 1) * 128]), inc=VE)
        for tb in range(NT1):
            t0b = tb * TT1
            P.wait("sync", VE)
            P.wait("sync", ST)
            P.dma("sync", gf1[:, :, :],
                  s_gf.rearrange("(g p) t -> p g t", p=128)[:, :, t0b:t0b + TT1], XL)
            P.dma("sync", xb1.ap().rearrange("p (n d) -> p n d", n=TT1 // 128),
                  dx.rearrange("(n p) d -> p n d", p=128)[:, t0b // 128: t0b // 128 + TT1 // 128, :], XL)
            xlv = P.cnt["XL"]
            P.wait("tensor", XL, xlv)
            for tcc4 in range(TT1 // 128):
                for q in range(2):
                    bk = bank[q % 2]
                    P.wait("tensor", VE)
                    for g2 in range(16):
                        P.mm(lambda e, tcc4=tcc4, q=q, g2=g2, bk=bk: e.matmul(
                            bk[:, :], gf1[:, g2, tcc4 * 128:(tcc4 + 1) * 128],
                            OWT[g2][:, q * 512:(q + 1) * 512],
                            start=(g2 == 0), stop=(g2 == 15)), inc=MM)
                    P.wait("vector", MM)
                    _po = pstage.ap()[:, tcc4 * DM + q * 512: tcc4 * DM + (q + 1) * 512]
                    _xo = xb1.ap()[:, tcc4 * DM + q * 512: tcc4 * DM + (q + 1) * 512]
                    P.do("vector", lambda e, bk=bk, po=_po, xo=_xo: e.tensor_tensor(
                        out=po, in0=bk[:, :], in1=xo, op=ALU.add), inc=VE)
            P.wait("sync", VE)
            P.dma("sync",
                  s_part.rearrange("(n p) d -> p n d", p=128)
                  [:, t0b // 128: t0b // 128 + TT1 // 128, :],
                  pstage.ap().rearrange("p (n d) -> p n d", n=TT1 // 128), ST)

    if stage >= 5:
        TT2 = min(1024, L)
        NT2 = L // TT2
        NC2 = TT2 // 128
        xnb = A("xnb", [128, 8, TT2], BF16)
        ptin = A("ptin", [128, NC2 * DM], BF16)
        accb = A("accb", [128, NC2 * DM], F32)
        ebuf = A("ebuf", [128, 2, 8, DM], BF16)
        rw_sb = A("rw_sb", [128, 8, NEXP], BF16)
        rbr = A("rbr", [128, NEXP], F32)
        eb_sb = A("eb_sb", [1, NEXP * DM], BF16)
        lgt = A("lgt", [128, NC2 * NEXP], F32)
        coefs = A("coefs", [128, NC2 * NEXP], F32)
        onesc = A("onesc", [128, NEXP], F32)
        msku = A("msku", [128, NEXP], U8)
        mxs = A("mxs", [128, NC2], F32)
        gts = A("gts", [128, NC2], F32)
        P.dma("gpsimd", rw_sb[:, :, :], din["rw"].rearrange("(g p) e -> p g e", p=128), LDG)
        P.dma("sync", rbr[:, :], din["rb_rep"][:, :], LD)
        P.dma("gpsimd", eb_sb[:, :], din["eb"].rearrange("e d -> (e d)").rearrange("(o n) -> o n", o=1), LDG)
        P.do("vector", lambda e: e.memzero(onesc[:, :]), inc=VB)
        P.wait("vector", VB)
        P.do("vector", lambda e: e.tensor_scalar(out=onesc[:, :], in0=onesc[:, :],
                                                 scalar1=1.0, scalar2=None, op0=ALU.add), inc=VE)
        ldv0 = P.wait("gpsimd", LDG)
        for g in range(8):
            P.dma("gpsimd", ebuf[:, 0, g, :], din["ew"][0, g * 128:(g + 1) * 128, :], LDG)
        for tb in range(NT2):
            t0b = tb * TT2
            P.wait("sync", VE)
            P.wait("sync", ST)
            P.dma("sync", xnb[:, :, :],
                  s_xnf.rearrange("(g p) t -> p g t", p=128)[:, :, t0b:t0b + TT2], XL)
            P.dma("sync", ptin.ap().rearrange("p (n d) -> p n d", n=NC2),
                  s_part.rearrange("(n p) d -> p n d", p=128)[:, t0b // 128: t0b // 128 + NC2, :], XL)
            xlv = P.cnt["XL"]
            P.wait("tensor", XL, xlv)
            P.wait("tensor", LD)
            for n2 in range(NC2):
                bk = bank[3]
                P.wait("tensor", VE)
                for g in range(8):
                    P.mm(lambda e, n2=n2, g=g, bk=bk: e.matmul(
                        bk[:, 0:NEXP], xnb[:, g, n2 * 128:(n2 + 1) * 128], rw_sb[:, g, :],
                        start=(g == 0), stop=(g == 7)), inc=MM)
                P.wait("vector", MM)
                P.do("vector", lambda e, n2=n2, bk=bk: e.tensor_tensor(
                    out=lgt[:, n2 * NEXP:(n2 + 1) * NEXP], in0=bk[:, 0:NEXP], in1=rbr[:, :],
                    op=ALU.add), inc=VE)
            P.wait("vector", VE)

            def _L8(n2):
                return lgt[:, n2 * NEXP:(n2 + 1) * NEXP]

            def _C8(n2):
                return coefs[:, n2 * NEXP:(n2 + 1) * NEXP]

            def _mx(n2):
                return mxs[:, n2:n2 + 1]

            def _gt(n2):
                return gts[:, n2:n2 + 1]
            for n2 in range(NC2):
                P.do("vector", lambda e, L8=_L8(n2), mx=_mx(n2): e.tensor_reduce(
                    out=mx, in_=L8, axis=mybir.AxisListType.X, op=ALU.max))
            for n2 in range(NC2):
                P.do("vector", lambda e, L8=_L8(n2), mx=_mx(n2): e.tensor_scalar(
                    out=L8, in0=L8, scalar1=mx, scalar2=None, op0=ALU.subtract),
                    inc=VE if n2 == NC2 - 1 else None)
            P.wait("scalar", VE)
            for n2 in range(NC2):
                P.do("scalar", lambda e, L8=_L8(n2): e.activation(
                    out=L8, in_=L8, func=AF.Exp), inc=AC if n2 == NC2 - 1 else None)
            P.wait("vector", AC)
            for n2 in range(NC2):
                P.do("vector", lambda e, L8=_L8(n2), mx=_mx(n2): e.tensor_reduce(
                    out=mx, in_=L8, axis=mybir.AxisListType.X, op=ALU.add))
            for n2 in range(NC2):
                P.do("vector", lambda e, mx=_mx(n2): e.reciprocal(out=mx, in_=mx))
            for n2 in range(NC2):
                P.do("vector", lambda e, L8=_L8(n2), mx=_mx(n2): e.tensor_scalar(
                    out=L8, in0=L8, scalar1=mx, scalar2=None, op0=ALU.mult))
            for n2 in range(NC2):
                P.do("vector", lambda e, L8=_L8(n2), gt=_gt(n2): e.tensor_reduce(
                    out=gt, in_=L8, axis=mybir.AxisListType.X, op=ALU.max))
            for n2 in range(NC2):
                P.do("vector", lambda e, L8=_L8(n2), gt=_gt(n2), C8=_C8(n2):
                     e.tensor_scalar(out=C8, in0=L8, scalar1=gt, scalar2=None,
                                     op0=ALU.is_ge))
            for n2 in range(NC2):
                P.do("vector", lambda e, C8=_C8(n2): e.tensor_tensor_scan(
                    out=C8, data0=onesc[:, :], data1=C8, initial=0.0,
                    op0=ALU.mult, op1=ALU.add),
                    inc=VB if n2 == NC2 - 1 else None)
            P.wait("vector", VB)
            for n2 in range(NC2):
                P.do("vector", lambda e, C8=_C8(n2): e.tensor_scalar(
                    out=C8, in0=C8, scalar1=1.0, scalar2=None, op0=ALU.is_le))
            for n2 in range(NC2):
                P.do("vector", lambda e, L8=_L8(n2), gt=_gt(n2), C8=_C8(n2):
                     e.scalar_tensor_tensor(out=C8, in0=L8, scalar=gt, in1=C8,
                                            op0=ALU.is_ge, op1=ALU.mult))
            for n2 in range(NC2):
                P.do("vector", lambda e, C8=_C8(n2), gt=_gt(n2): e.tensor_scalar(
                    out=C8, in0=C8, scalar1=gt, scalar2=None, op0=ALU.mult),
                    inc=VE if n2 == NC2 - 1 else None)
            P.wait("vector", VE)
            for n2 in range(NC2):
                P.do("vector", lambda e, n2=n2: e.tensor_copy(
                    out=accb[:, n2 * DM:(n2 + 1) * DM], in_=ptin.ap()[:, n2 * DM:(n2 + 1) * DM]),
                    inc=VE)
            for ee in range(NEXP):
                pb = ee % 2
                P.wait("tensor", LDG)
                nxt = None
                if ee + 1 < NEXP:
                    nxt = (tb, ee + 1)
                elif tb + 1 < NT2:
                    nxt = (tb + 1, 0)
                if nxt is not None:
                    P.wait("gpsimd", MM)
                    for g in range(8):
                        P.dma("gpsimd", ebuf[:, (ee + 1) % 2, g, :],
                              din["ew"][nxt[1], g * 128:(g + 1) * 128, :], LDG)
                for n2 in range(NC2):
                    for q in range(2):
                        bk = bank[4 + q]
                        P.wait("tensor", VE)
                        for g in range(8):
                            P.mm(lambda e, n2=n2, q=q, g=g, bk=bk, pb=pb: e.matmul(
                                bk[:, :], xnb[:, g, n2 * 128:(n2 + 1) * 128],
                                ebuf[:, pb, g, q * 512:(q + 1) * 512],
                                start=(g == 0), stop=False), inc=MM)
                        P.mm(lambda e, q=q, bk=bk, ee=ee: e.matmul(
                            bk[:, :], ones1[:, :],
                            eb_sb[0:1, ee * DM + q * 512: ee * DM + (q + 1) * 512],
                            start=False, stop=True), inc=MM)
                        P.wait("vector", MM)
                        _ac = accb.ap()[:, n2 * DM + q * 512: n2 * DM + (q + 1) * 512]
                        _cf = coefs[:, n2 * NEXP + ee: n2 * NEXP + ee + 1]
                        P.do("vector", lambda e, bk=bk, ac=_ac, cf=_cf: e.scalar_tensor_tensor(
                            out=ac, in0=bk[:, :], scalar=cf, in1=ac,
                            op0=ALU.mult, op1=ALU.add), inc=VE)
            P.wait("sync", VE)
            P.dma("sync",
                  d_out.rearrange("(n p) d -> p n d", p=128)[:, t0b // 128: t0b // 128 + NC2, :],
                  accb.ap().rearrange("p (n d) -> p n d", n=NC2), ST)

    for s_ in sorted(P.dma_sems):
        P.wait("sync", s_)
        P.wait("gpsimd", s_)

    with nc.Block() as block:
        sems = {}
        ctxs = []
        for name in P.cnt:
            cm = nc.semaphore(name)
            sems[name] = cm.__enter__()
            ctxs.append(cm)
        P.sems = sems
        P.build(block)

    return nc, dbg


def kernel(**inputs):
    x = np.ascontiguousarray(np.asarray(inputs["L0_attn_x"], dtype=np.float32))
    B, L, _ = x.shape
    consts = host_prepare(inputs)
    nc, _ = build_program(L=L)
    in_maps = []
    for b in range(B):
        m = dict(consts)
        m["x"] = x[b]
        in_maps.append(m)
    r = run_bass_kernel_spmd(nc, in_maps, core_ids=list(range(B)))
    return np.stack([r.results[b]["out"] for b in range(B)], axis=0)
